# version 14
# speedup vs baseline: 1.0378x; 1.0078x over previous
import os
import numpy as np
import ml_dtypes
from contextlib import ExitStack

import concourse.bass as bass
ALLOCS = {}
import concourse.mybir as mybir
from concourse.bass_utils import run_bass_kernel_spmd

F32 = mybir.dt.float32
BF16 = mybir.dt.bfloat16
I32 = mybir.dt.int32
AF = mybir.ActivationFunctionType
ALU = mybir.AluOpType
AX = mybir.AxisListType
BF = ml_dtypes.bfloat16

T = 4096
DM = 1024
DI = 2048
NT = T // 128
NRUN = T // 512
EPS = 1e-5

_DBG = False

ENGINES = ("tensor", "vector", "scalar", "sync", "gpsimd")


def host_constants():
    r = np.arange(128)
    tri_up = (r[:, None] <= r[None, :]).astype(np.float32)
    tris = (r[:, None] > r[None, :]).astype(np.float32)
    postri4 = np.tile((r[:, None] > r[None, :]) * np.float32(-1e30), (1, 4))
    e32 = np.zeros((32, 2048), np.float32)
    for h in range(32):
        e32[h, h * 64:(h + 1) * 64] = 1.0
    e16 = np.zeros((16, 1024), np.float32)
    for h in range(16):
        e16[h, h * 64:(h + 1) * 64] = 1.0
    e8 = np.zeros((32, 16), np.float32)
    for h in range(32):
        e8[h, h // 2] = 1.0
    pairsel = np.zeros((32, 128), np.float32)
    for h in range(32):
        pairsel[h, (h % 2) * 64:(h % 2) * 64 + 64] = 1.0
    return {
        "c_ident_bf": np.eye(128, dtype=BF),
        "c_ident_f": np.eye(128, dtype=np.float32),
        "c_tri_up_f": tri_up,
        "c_tri_up_bf": tri_up.astype(BF),
        "c_tris_bf": tris.astype(BF),
        "c_postri4_bf": postri4.astype(BF),
        "c_e32_bf": e32.astype(BF),
        "c_e32_f": e32,
        "c_e16_f": e16,
        "c_e8_f": e8,
        "c_pairsel_f": pairsel,
        "c_onesall_bf": np.ones((128, 128), dtype=BF),
        "c_e127_bf": np.eye(128, dtype=BF)[:, 127:128].copy(),
        "c_cvals": np.tile(np.array([[EPS, 1.0, -1.0, -0.5]], np.float32),
                           (128, 1)),
        "c_ones_f": np.ones((1, 128), np.float32),
        "c_ones32_f": np.ones((32, 128), np.float32),
        "c_zeros32_f": np.zeros((32, 128), np.float32),
    }


class Sem:
    __slots__ = ("h", "n", "name")

    def __init__(self, h, name):
        self.h, self.n, self.name = h, 0, name


class _Stop(Exception):
    pass


def build(dbg=False, upto=None):
    nc = bass.Bass("TRN2", target_bir_lowering=False,
                   detect_race_conditions=False)

    def I(name, shape, dt=F32):
        return nc.dram_tensor(name, shape, dt, kind="ExternalInput").ap()

    def SCR(name, shape, dt):
        kind = "ExternalOutput" if dbg else "Internal"
        return nc.dram_tensor(name, shape, dt, kind=kind).ap()

    x_in = I("x", [T, DM])
    w_in = I("in_proj_w", [2 * DI, DM])
    conv_w = I("conv_w", [DI, 4])
    conv_b = I("conv_b", [DI])
    a_log = I("A_log", [32])
    dt_w = I("dt_w", [32, DI])
    dt_b = I("dt_b", [32])
    b_w = I("B_w", [DI, DI])
    c_w = I("C_w", [DI, DI])
    d_in = I("D", [32])
    ln_w = I("ln_w", [DI])
    ln_b = I("ln_b", [DI])
    w_out = I("out_proj_w", [DM, DI])
    f_gam = I("ffn_ln_gamma", [DM])
    f_bet = I("ffn_ln_beta", [DM])
    fc1_q = I("fc1_q", [DM, 4096], I32)
    fc1_s = I("fc1_scales", [8, 4096])
    fc1_b = I("fc1_bias", [4096])
    fc2_q = I("fc2_q", [4096, DM], I32)
    fc2_s = I("fc2_scales", [32, DM])
    fc2_b = I("fc2_bias", [DM])
    wqkv = I("Wqkv", [3 * DM, DM])
    bqkv = I("bqkv", [3 * DM])
    wproj = I("Wproj", [DM, DM])
    bproj = I("bproj", [DM])
    scale_in = I("scale", [1, 1])
    l1w = I("L1_weight", [DM, DM])
    disha = I("disha", [64, DM])
    cns = {nm: I(nm, list(arr.shape), BF16 if arr.dtype == BF else F32)
           for nm, arr in host_constants().items()}
    out = nc.dram_tensor("out", [T, DM], F32, kind="ExternalOutput").ap()

    xact_sp = SCR("xact_sp", [16, 128, T], BF16)
    z_sp = SCR("z_sp", [T, DI], F32)
    y_sp = SCR("y_sp", [T, DI], F32)
    r1_sp = SCR("r1_sp", [T, DM], F32)
    r2_sp = SCR("r2_sp", [T, DM], F32)
    r3_sp = SCR("r3_sp", [T, DM], F32)
    bwt_sp = SCR("bwt_sp", [16, 128, DI], BF16)
    dbc_sp = SCR("dbc_sp", [128, DI], F32)
    cwt_sp = SCR("cwt_sp", [16, 128, DI], BF16)
    dbg_dt = SCR("dbg_dt", [32, T], F32) if dbg else None

    ARENA = (nc.sbuf_bytes_remaining - 512) & ~511
    _arena = nc.alloc_sbuf_tensor("karena", [128, ARENA], mybir.dt.uint8)
    _base = nc.lookup_mloc(_arena).addr
    SBUF_LIMIT = _base + ARENA
    res_off = [_base]
    ph_base = [0]
    ph_off = [0]

    def _alloc(name, shape, dt, off_list):
        per = int(np.prod(shape[1:])) * mybir.dt.size(dt)
        o = (off_list[0] + 31) & ~31
        ALLOCS[name] = (o - _base, list(shape), dt)
        h = nc.alloc_sbuf_tensor_at(name, list(shape), dt, offset=o)
        off_list[0] = o + per
        assert off_list[0] <= SBUF_LIMIT, (
            f"SBUF overflow at {name}: {off_list[0]} > {SBUF_LIMIT}")
        return h

    def R(name, shape, dt=F32):
        assert ph_base[0] == 0, "resident alloc after phases started"
        return _alloc(name, shape, dt, res_off)

    ph_peak = {}
    ph_name = ["?"]

    def PH(name, shape, dt=F32):
        h = _alloc(name, shape, dt, ph_off)
        ph_peak[ph_name[0]] = max(ph_peak.get(ph_name[0], 0), ph_off[0])
        return h

    barrier_hook = [None]

    def phase_reset(nm="?"):
        if ph_base[0] == 0:
            ph_base[0] = res_off[0]
        elif barrier_hook[0] is not None:
            barrier_hook[0]()
        ph_off[0] = ph_base[0]
        ph_name[0] = nm

    es = ExitStack()

    def psum(name):
        return es.enter_context(nc.psum_tensor(name, [128, 512], F32))

    def mksem(name):
        return Sem(es.enter_context(nc.semaphore(name)), name)

    q = {e: [] for e in ENGINES}

    def on(eng, fn, inc=None, chain=True):
        incs = []
        if inc:
            for s, k in inc:
                s.n += k
                incs.append((s, k))
        q[eng].append((fn, incs,
                       eng in ("vector", "scalar") and chain, True))

    def wait(eng, sem, val):
        if val > 0:
            q[eng].append((lambda E, s=sem, v=val: E.wait_ge(s.h, v), [],
                           False, False))

    def wait_all(eng, lst):
        for s, v in lst:
            if s is not None:
                wait(eng, s, v)

    def stop_if(nm):
        if upto == nm:
            raise _Stop()

    with es:
        sT = mksem("sT")
        sV = mksem("sV")
        sS = mksem("sS")
        SEM = {"vector": sV, "scalar": sS, "tensor": sT}
        dsems = {}

        def dma(qe, o, i, tag, slow=False):
            if tag not in dsems:
                dsems[tag] = mksem("d_" + tag)
            sem = dsems[tag]
            on(qe, lambda E, o=o, i=i, slow=slow: E.dma_start(
                out=o, in_=i, allow_slow_non_contiguous=slow), inc=[(sem, 16)])
            return (sem, sem.n)

        def dwait(eng, sv):
            wait(eng, sv[0], sv[1])

        def dma_barrier():
            for e in ENGINES:
                for s_ in (sT, sV, sS):
                    wait(e, s_, s_.n)
                for s_ in dsems.values():
                    wait(e, s_, s_.n)
        barrier_hook[0] = dma_barrier

        class Bank:
            def __init__(self, ps, name):
                self.ps, self.name = ps, name
                self.w = None
                self.rs = []

            def mm(self, mms):
                for s, v in self.rs:
                    wait("tensor", s, v)
                self.rs = []
                for i, fn in enumerate(mms):
                    on("tensor", fn, inc=[(sT, 1)] if i == len(mms) - 1 else None)
                self.w = (sT, sT.n)

            def read(self, eng, fn, chain=True):
                wait(eng, self.w[0], self.w[1])
                on(eng, fn, inc=[(SEM[eng], 1)], chain=chain)
                self.rs.append((SEM[eng], SEM[eng].n))

        banks = [Bank(psum(f"pb{i}"), f"pb{i}") for i in range(8)]

        ident_bf = R("ident_bf", [128, 128], BF16)
        ident_f = R("ident_f", [128, 128], F32)
        tri_up_f = R("tri_up_f", [128, 128], F32)
        tri_up_bf = R("tri_up_bf", [128, 128], BF16)
        tris_bf = R("tris_bf", [128, 128], BF16)
        postri4 = R("postri4", [128, 512], BF16)
        e32bf = R("e32bf", [32, 2048], BF16)
        e8f = R("e8f", [32, 16], F32)
        pairself = R("pairself", [32, 128], F32)
        onesall_bf = R("onesall_bf", [128, 128], BF16)
        e127_bf = R("e127_bf", [128, 1], BF16)
        cvals = R("cvals", [128, 4], F32)
        for _i, _v in enumerate([EPS, 1.0, -1.0, -0.5]):
            nc.const_aps.register(F32, _v, cvals[:, _i:_i + 1])
        ones_f = R("ones_f", [1, 128], F32)
        ones32f = R("ones32f", [32, 128], F32)
        zeros32f = R("zeros32f", [32, 128], F32)
        convw_sb = R("convw_sb", [128, 64], F32)
        convb_sb = R("convb_sb", [128, 16], F32)
        dtb_col = R("dtb_col", [32, 1], F32)
        a_col = R("a_col", [32, 1], F32)
        a128 = R("a128", [128, 32], F32)
        dcol = R("dcol", [32, 1], F32)
        dtwT = R("dtwT", [128, 512], BF16)
        scale_col = R("scale_col", [128, 1], F32)
        fc1b_col = R("fc1b_col", [128, 32], F32)
        st_sum = R("st_sum", [128, 32], F32)
        st_sq = R("st_sq", [128, 32], F32)
        st_rstd = R("st_rstd", [128, 32], F32)
        st_nb = R("st_nb", [128, 32], F32)
        st_comb = R("st_comb", [128, 32], F32)
        st_nbc = R("st_nbc", [128, 32], F32)
        sq_scr = R("sq_scr", [128, 2048], F32)
        ccol = R("ccol", [128, 8], F32)

        def dma_tr(out_ap, in_ap, tag):
            if tag not in dsems:
                dsems[tag] = mksem("d_" + tag)
            sem = dsems[tag]
            on("sync", lambda E, o=out_ap, i=in_ap: E.dma_start(
                out=o, in_=i, transpose=True), inc=[(sem, 16)])
            return (sem, sem.n)

        def transpose_to(src_ap, bank, dst_fn, n=128, chain=False):
            bank.mm([lambda E, s=src_ap, b=bank: E.matmul(
                b.ps[:, 0:n], s, ident_bf[0:src_ap.shape[0], 0:n],
                start=True, stop=True)])
            bank.read("vector", lambda E, b=bank: dst_fn(E, b), chain=chain)

        def ln_stats_tile(src_tile, width, tt, wsum, wsq):
            on("vector", lambda E, b=src_tile, tt=tt, wsum=wsum: E.tensor_reduce(
                out=wsum[:, tt:tt + 1], in_=b, axis=AX.X, op=ALU.add),
                inc=[(sV, 1)])
            wait("scalar", sV, sV.n)
            on("scalar", lambda E, b=src_tile, tt=tt, wsq=wsq:
                E.activation(out=b, in_=b, func=AF.Square,
                             accum_out=wsq[:, tt:tt + 1]), inc=[(sS, 1)])

        def ln_finalize(width, ncols=32, derive_u=False):
            inv = 1.0 / width
            c = slice(0, ncols)
            on("vector", lambda E: E.tensor_scalar(
                out=st_sum[:, c], in0=st_sum[:, c], scalar1=inv, scalar2=None,
                op0=ALU.mult))
            on("vector", lambda E: E.tensor_tensor(
                out=st_rstd[:, c], in0=st_sum[:, c], in1=st_sum[:, c], op=ALU.mult))
            on("vector", lambda E: E.scalar_tensor_tensor(
                out=st_sq[:, c], in0=st_sq[:, c], scalar=inv, in1=st_rstd[:, c],
                op0=ALU.mult, op1=ALU.subtract), inc=[(sV, 1)])
            wait("scalar", sV, sV.n)
            on("scalar", lambda E: E.activation(
                out=st_rstd[:, c], in_=st_sq[:, c], func=AF.Ln, bias=EPS))
            on("scalar", lambda E: E.activation(
                out=st_rstd[:, c], in_=st_rstd[:, c], func=AF.Exp, scale=-0.5),
                inc=[(sS, 1)])
            wait("vector", sS, sS.n)
            on("vector", lambda E: E.scalar_tensor_tensor(
                out=st_nb[:, c], in0=st_sum[:, c], scalar=-1.0, in1=st_rstd[:, c],
                op0=ALU.mult, op1=ALU.mult), inc=[(sV, 1)])
            if derive_u:
                on("vector", lambda E: E.tensor_tensor(
                    out=st_comb[:, c], in0=st_rstd[:, c], in1=st_rstd[:, c],
                    op=ALU.mult))
                on("vector", lambda E: E.tensor_tensor(
                    out=st_comb[:, c], in0=st_comb[:, c], in1=st_sq[:, c],
                    op=ALU.mult), inc=[(sV, 1)])
                wait("scalar", sV, sV.n)
                on("scalar", lambda E: E.activation(
                    out=st_comb[:, c], in_=st_comb[:, c], func=AF.Ln, bias=EPS))
                on("scalar", lambda E: E.activation(
                    out=st_comb[:, c], in_=st_comb[:, c], func=AF.Exp, scale=-0.5),
                    inc=[(sS, 1)])
                wait("vector", sS, sS.n)
                on("vector", lambda E: E.tensor_tensor(
                    out=st_comb[:, c], in0=st_comb[:, c], in1=st_rstd[:, c],
                    op=ALU.mult))
                on("vector", lambda E: E.scalar_tensor_tensor(
                    out=st_nbc[:, c], in0=st_sum[:, c], scalar=-1.0,
                    in1=st_comb[:, c], op0=ALU.mult, op1=ALU.mult), inc=[(sV, 1)])

        def bcast_row_f32(row_ap, tgt, width, bank_pick):
            for i in range(width // 512):
                b = banks[bank_pick(i)]
                b.mm([lambda E, b=b, i=i, r=row_ap: E.matmul(
                    b.ps[:], ones_f[:], r[0:1, i * 512:(i + 1) * 512],
                    start=True, stop=True)])
                b.read("vector", lambda E, b=b, i=i, t=tgt: E.tensor_copy(
                    out=t[:, i * 512:(i + 1) * 512], in_=b.ps[:]), chain=False)

        phase_reset("p0")
        e32f = PH("e32f", [32, 2048], F32)
        arow = PH("arow", [1, 32], F32)
        drow = PH("drow", [1, 32], F32)
        dexrow = PH("dexrow", [1, DI], F32)
        alog_c = PH("alog_c", [32, 1], F32)
        dtw_bf = PH("dtw_bf", [32, DI], BF16)
        p0wrow = [PH(f"p0wrow{i}", [128, DI], BF16) for i in range(2)]
        p0slab = [PH(f"p0slab{k}", [128, DI], BF16) for k in range(16)]

        for nm, t_sb in [("c_ident_bf", ident_bf), ("c_ident_f", ident_f),
                         ("c_tri_up_f", tri_up_f), ("c_tri_up_bf", tri_up_bf),
                         ("c_tris_bf", tris_bf), ("c_postri4_bf", postri4),
                         ("c_e32_bf", e32bf), ("c_e32_f", e32f),
                         ("c_e8_f", e8f), ("c_pairsel_f", pairself),
                         ("c_onesall_bf", onesall_bf), ("c_ones_f", ones_f),
                         ("c_e127_bf", e127_bf), ("c_cvals", cvals),
                         ("c_ones32_f", ones32f), ("c_zeros32_f", zeros32f)]:
            dma("gpsimd", t_sb[:], cns[nm][:], "p0s")
        dma("gpsimd", convw_sb[:].rearrange("p (c k) -> p c k", k=4),
            conv_w.rearrange("(c p) k -> p c k", p=128), "p0s")
        dma("gpsimd", convb_sb[:], conv_b.rearrange("(c p) -> p c", p=128), "p0s", slow=True)
        dma("gpsimd", dtb_col[:], dt_b[:, None], "p0s")
        dma("gpsimd", alog_c[:], a_log[:, None], "p0s")
        dma("gpsimd", dcol[:], d_in[:, None], "p0s")
        dma("gpsimd", scale_col[0:1, :], scale_in[:], "p0s")
        dma("gpsimd", fc1b_col[:], fc1_b.rearrange("(c p) -> p c", p=128), "p0s", slow=True)
        dma("gpsimd", dtw_bf[:], dt_w[:], "p0s")
        n_small = ("p0s", dsems["p0s"].n)

        wait("scalar", dsems["p0s"], n_small[1])
        on("scalar", lambda E: E.activation(out=a_col[:], in_=alog_c[:],
                                            func=AF.Exp), inc=[(sS, 1)])
        wait("vector", sS, sS.n)
        on("vector", lambda E: E.tensor_scalar(
            out=a_col[:], in0=a_col[:], scalar1=-1.0, scalar2=None, op0=ALU.mult),
            inc=[(sV, 1)])
        wait("tensor", sV, sV.n)
        wait("tensor", dsems["p0s"], n_small[1])
        b0, b1 = banks[0], banks[1]
        b0.mm([lambda E: E.matmul(b0.ps[0:1, 0:128], a_col[:], ident_f[0:32, :],
                                  start=True, stop=True),
               lambda E: E.matmul(b0.ps[0:1, 128:256], dcol[:], ident_f[0:32, :],
                                  start=True, stop=True)])
        b0.read("vector", lambda E: E.tensor_copy(out=arow[:], in_=b0.ps[0:1, 0:32]))
        b0.read("vector", lambda E: E.tensor_copy(out=drow[:], in_=b0.ps[0:1, 128:160]))
        b1.mm([lambda E: E.matmul(b1.ps[:, 0:1], ones_f[:], scale_col[0:1, :],
                                  start=True, stop=True)])
        b1.read("vector", lambda E: E.tensor_copy(out=scale_col[:], in_=b1.ps[:, 0:1]))
        wait("tensor", sV, sV.n)
        b0.mm([lambda E: E.matmul(b0.ps[:, 0:32], ones_f[:], arow[:],
                                  start=True, stop=True)])
        b0.read("vector", lambda E: E.tensor_copy(out=a128[:], in_=b0.ps[:, 0:32]))
        for i in range(4):
            bb = banks[2 + i % 2]
            bb.mm([lambda E, bb=bb, i=i: E.matmul(
                bb.ps[0:1, 0:512], dcol[:], e32f[:, i * 512:(i + 1) * 512],
                start=True, stop=True)])
            bb.read("vector", lambda E, bb=bb, i=i: E.tensor_copy(
                out=dexrow[0:1, i * 512:(i + 1) * 512], in_=bb.ps[0:1, 0:512]))
        dbc0 = PH("dbc0", [128, DI], F32)
        wait("tensor", sV, sV.n)
        bcast_row_f32(dexrow, dbc0, DI, lambda i: 6 + i % 2)
        wait("sync", sV, sV.n)
        dma("sync", dbc_sp[:], dbc0[:], "p0slab")
        wait("tensor", sV, sV.n)
        for k in range(16):
            bb = banks[4 + k % 2]
            bb.mm([lambda E, bb=bb, k=k: E.matmul(
                bb.ps[:, 0:128], dtw_bf[:, k * 128:(k + 1) * 128],
                ident_bf[0:32, :], start=True, stop=True)])
            bb.read("vector", lambda E, bb=bb, k=k: E.tensor_copy(
                out=dtwT[:, k * 32:(k + 1) * 32], in_=bb.ps[:, 0:32]))
        rowfree = [(None, 0), (None, 0)]
        for wi, (wsrc, wdst) in enumerate([(b_w, bwt_sp), (c_w, cwt_sp)]):
            slabfree = dsems["p0slab"].n if "p0slab" in dsems else 0
            for nf in range(16):
                buf = p0wrow[nf % 2]
                s, v = rowfree[nf % 2]
                if s is not None:
                    wait("gpsimd", s, v)
                nld = dma("gpsimd", buf[:], wsrc[nf * 128:(nf + 1) * 128, :],
                          f"p0w{nf % 2}")
                dwait("tensor", nld)
                if nf == 0 and wi > 0:
                    wait("tensor", dsems["p0slab"], slabfree)
                for k in range(16):
                    bb = banks[k % 4]
                    bb.mm([lambda E, bb=bb, buf=buf, k=k: E.matmul(
                        bb.ps[:, 0:128], buf[:, k * 128:(k + 1) * 128],
                        ident_bf[:], start=True, stop=True)])
                    bb.read("vector", lambda E, bb=bb, k=k, nf=nf: E.tensor_copy(
                        out=p0slab[k][:, nf * 128:(nf + 1) * 128],
                        in_=bb.ps[:, 0:128]))
                rowfree[nf % 2] = (sT, sT.n)
            wait("sync", sV, sV.n)
            for k in range(16):
                dma("sync", wdst[k], p0slab[k][:], "p0slab")

        phase_reset("ma")
        x_lnfm = PH("x_lnfm", [128, NT * 1024], BF16)
        marb = [PH(f"marb{i}", [128, DM], F32) for i in range(3)]
        maxl = [PH(f"maxl{i}", [128, DM], BF16) for i in range(2)]
        mawr = [PH(f"mawr{i}", [128, DM], BF16) for i in range(2)]
        mawt = [PH(f"mawt{i}", [128, 1024], BF16) for i in range(2)]
        xpre = PH("xpre", [128, T + 3], BF16)
        xactb = PH("xactb", [128, T], BF16)
        diag = PH("diag", [128, 512], BF16)
        zmov = PH("zmov", [128, 8 * 512], BF16)
        zev = [PH(f"zev{i}", [128, 512], F32) for i in range(2)]

        free3 = [[] for _ in range(3)]
        for tt in range(NT):
            b = marb[tt % 3]
            wait_all("gpsimd", free3[tt % 3])
            nld = dma("gpsimd", b[:], x_in[tt * 128:(tt + 1) * 128, :],
                      f"marb{tt % 3}")
            dwait("vector", nld)
            dwait("scalar", nld)
            ln_stats_tile(b[:], DM, tt, st_sum, st_sq)
            free3[tt % 3] = [(sS, sS.n), (sV, sV.n)]
        wait("vector", sS, sS.n)
        ln_finalize(DM)
        nfin_v, nfin_s = sV.n, sS.n

        free2 = [(None, 0)] * 2
        for tt in range(NT):
            b = marb[tt % 3]
            wait_all("gpsimd", free3[tt % 3])
            nld = dma("gpsimd", b[:], x_in[tt * 128:(tt + 1) * 128, :],
                      f"marb{tt % 3}")
            xb = maxl[tt % 2]
            s, v = free2[tt % 2]
            if s is not None:
                wait("scalar", s, v)
            dwait("scalar", nld)
            wait("scalar", sV, nfin_v)
            on("scalar", lambda E, b=b, xb=xb, tt=tt: E.activation(
                out=xb[:], in_=b[:], func=AF.Identity, scale=st_rstd[:, tt:tt + 1],
                bias=st_nb[:, tt:tt + 1]), inc=[(sS, 1)])
            free3[tt % 3] = [(sS, sS.n)]
            wait("tensor", sS, sS.n)
            for half in range(2):
                bb = banks[half]
                bb.mm([(lambda E, xb=xb, c=half * 4 + c4, c4=c4, bb=bb: E.matmul(
                    bb.ps[:, c4 * 128:(c4 + 1) * 128],
                    xb[:, c * 128:(c + 1) * 128], ident_bf[:],
                    start=True, stop=True)) for c4 in range(4)])
                bb.read("vector", lambda E, tt=tt, half=half, bb=bb: E.tensor_copy(
                    out=x_lnfm[:, tt * 1024 + half * 512: tt * 1024 + (half + 1) * 512],
                    in_=bb.ps[:]))
            free2[tt % 2] = (sT, sT.n)
        n_xlnfm = sV.n
        xl4 = x_lnfm[:].rearrange("p (t c i) -> p t c i", c=8, i=128)

        freew = [(None, 0)] * 2
        freewt = [(None, 0)] * 2
        on("vector", lambda E: E.memset(xpre[:, 0:3], 0.0), inc=[(sV, 1)])
        xact_store = None
        for ob in range(16):
            wb = mawr[ob % 2]
            s, v = freew[ob % 2]
            if s is not None:
                wait("gpsimd", s, v)
            nld = dma("gpsimd", wb[:], w_in[ob * 128:(ob + 1) * 128, :],
                      f"mawr{ob % 2}")
            wt = mawt[ob % 2]
            s, v = freewt[ob % 2]
            if s is not None:
                wait("tensor", s, v)
            dwait("tensor", nld)
            wait("tensor", sV, n_xlnfm)
            for k in range(8):
                transpose_to(wb[:, k * 128:(k + 1) * 128], banks[2 + k % 2],
                             lambda E, b, wt=wt, k=k: E.tensor_copy(
                                 out=wt[:, k * 128:(k + 1) * 128],
                                 in_=b.ps[:, 0:128]))
            freew[ob % 2] = (sT, sT.n)
            nwt = sV.n
            wait("tensor", sV, nwt)
            for r in range(NRUN):
                bb = banks[4 + r % 2]
                bb.mm([(lambda E, wt=wt, k=k, r=r, bb=bb: E.matmul(
                    bb.ps[:], wt[:, k * 128:(k + 1) * 128],
                    xl4[:, r * 4:(r + 1) * 4, k, :],
                    start=(k == 0), stop=(k == 7))) for k in range(8)])
                bb.read("vector", lambda E, r=r, bb=bb: E.tensor_copy(
                    out=xpre[:, 3 + r * 512: 3 + (r + 1) * 512], in_=bb.ps[:]),
                        chain=False)
            for k in range(4):
                on("vector", lambda E, ob=ob, k=k: E.tensor_scalar(
                    out=diag[:, k * 128:(k + 1) * 128], in0=ident_bf[:],
                    scalar1=convw_sb[:, ob * 4 + k: ob * 4 + k + 1],
                    scalar2=None, op0=ALU.mult), inc=[(sV, 1)] if k == 3 else None,
                    chain=(k == 0))
            ndiag = sV.n
            wait("tensor", sV, ndiag)
            if xact_store is not None:
                dwait("scalar", xact_store)
            for r in range(NRUN):
                bb = banks[6 + r % 2]
                bb.mm([(lambda E, k=k, r=r, bb=bb: E.matmul(
                    bb.ps[:], diag[:, k * 128:(k + 1) * 128],
                    xpre[:, r * 512 + k: r * 512 + k + 512],
                    start=(k == 0), stop=(k == 3))) for k in range(4)])
                bb.read("scalar", lambda E, ob=ob, r=r, bb=bb: E.activation(
                    out=xactb[:, r * 512:(r + 1) * 512], in_=bb.ps[:],
                    func=AF.Silu, bias=convb_sb[:, ob:ob + 1]), chain=(r == 0))
            wait("sync", sS, sS.n)
            xact_store = dma("sync", xact_sp[ob], xactb[:], "xact_st")
            freewt[ob % 2] = (sT, sT.n)

        zfree = [(None, 0)] * 2
        for og in range(4):
            for obl in range(4):
                ob = 16 + og * 4 + obl
                wb = mawr[obl % 2]
                s, v = freew[obl % 2]
                if s is not None:
                    wait("gpsimd", s, v)
                nld = dma("gpsimd", wb[:], w_in[ob * 128:(ob + 1) * 128, :],
                          f"mawr{obl % 2}")
                dwait("tensor", nld)
                for k in range(8):
                    transpose_to(wb[:, k * 128:(k + 1) * 128], banks[k % 2],
                                 lambda E, b, k=k, obl=obl: E.tensor_copy(
                                     out=zmov[:, k * 512 + obl * 128:
                                              k * 512 + (obl + 1) * 128],
                                     in_=b.ps[:, 0:128]))
                freew[obl % 2] = (sT, sT.n)
            nzm = sV.n
            wait("tensor", sV, nzm)
            for tt in range(NT):
                bb = banks[2 + tt % 2]
                zb = zev[tt % 2]
                s, v = zfree[tt % 2]
                if s is not None:
                    wait("vector", s, v)
                bb.mm([(lambda E, tt=tt, k=k, bb=bb: E.matmul(
                    bb.ps[:],
                    x_lnfm[:, tt * 1024 + k * 128: tt * 1024 + (k + 1) * 128],
                    zmov[:, k * 512:(k + 1) * 512],
                    start=(k == 0), stop=(k == 7))) for k in range(8)])
                bb.read("vector", lambda E, zb=zb, bb=bb: E.tensor_copy(
                    out=zb[:], in_=bb.ps[:]), chain=False)
                wait("sync", sV, sV.n)
                zfree[tt % 2] = dma(
                    "sync", z_sp[tt * 128:(tt + 1) * 128, og * 512:(og + 1) * 512],
                    zb[:], f"zst{tt % 2}")

        dma_barrier()

        phase_reset("pm")
        bwt_half = PH("bwt_half", [128, 16 * 1024], BF16)
        cwt_half = PH("cwt_half", [128, 16 * 1024], BF16)
        xar = PH("xar", [128, 16 * 512], BF16)
        dt_all = PH("dt_all", [32, T], F32)
        dt_tmp = PH("dt_tmp", [32, 512], F32)
        bds2 = [PH(f"bds{j}", [128, 8 * 512], BF16) for j in range(2)]
        cds2 = [PH(f"cds{j}", [128, 8 * 512], BF16) for j in range(2)]
        hstate = PH("hstate", [128, 512], F32)
        hbd = PH("hbd", [128, 1024], BF16)
        cdsbd = PH("cdsbd", [128, 2048], BF16)
        lam = PH("lam", [32, 128], F32)
        wvec = PH("wvec", [32, 128], F32)
        wbf = PH("wbf", [32, 128], BF16)
        e8lam = PH("e8lam", [32, 8], F32)
        adt_hm = PH("adt_hm", [32, 128], F32)
        adtT = PH("adtT", [128, 32], F32)
        adt_bf = PH("adt_bf", [128, 32], BF16)
        adthi_f = PH("adthi_f", [128, 32], F32)
        adtlo_f = PH("adtlo_f", [128, 32], F32)
        explam_hm = PH("explam_hm", [32, 128], BF16)
        dtmgH = PH("dtmgH", [128, 2048], BF16)
        dtmgL = PH("dtmgL", [128, 2048], BF16)
        expDb = nc.alloc_sbuf_tensor_at("expDb_al", [128, 2048], BF16,
                                        offset=ALLOCS["dtmgH"][0] + _base)
        dtbb = PH("dtbb", [128, 1024], BF16)
        wtbb = PH("wtbb", [128, 1024], BF16)
        _sqoff = ALLOCS["sq_scr"][0] + _base
        ybuf = nc.alloc_sbuf_tensor_at("ybuf_al", [128, 1024], F32,
                                       offset=_sqoff)
        dxbuf = nc.alloc_sbuf_tensor_at("dxbuf_al", [128, 1024], F32,
                                        offset=_sqoff + 4096)
        dthi_c = PH("dthi_c", [32, 128], BF16)
        dbc_h = PH("dbc_h", [128, 1024], F32)
        btok2 = [PH(f"btok{j}", [128, 1024], BF16) for j in range(2)]
        xtok2 = [PH(f"xtok{j}", [128, 1024], BF16) for j in range(2)]
        xd2 = [PH(f"xd{j}", [128, 1024], BF16) for j in range(2)]
        xw2 = [PH(f"xw{j}", [128, 1024], BF16) for j in range(2)]
        smg2 = [PH(f"smg{j}", [128, 2048], BF16) for j in range(2)]
        explamb2 = [PH(f"explamb{j}", [128, 1024], BF16) for j in range(2)]
        expLp2 = [PH(f"expLp{j}", [128, 8], F32) for j in range(2)]
        on("vector", lambda E: E.memset(cdsbd[:], 0.0))
        on("vector", lambda E: E.memset(hbd[:], 0.0), inc=[(sV, 1)])

        st = {"pe_par": 0, "nbds": [0, 0], "ncds": [0, 0], "xarfree": 0,
              "hfree": 0, "nSM": 0,
              "wexp": 0,
              "ebfree": 0,
              "trB": [None, None], "trX": [None, None],
              "trfree": [[0, 0], [0, 0]], "nexpL": [0, 0],
              "nSMv": [0, 0], "Ffree": [0, 0]}

        def ssd_par(half, r, c):
            j = c % 2
            rp = r % 2
            bds, cds = bds2[rp], cds2[rp]
            t0 = r * 512 + c * 128
            sl = slice(t0, t0 + 128)
            btj, xtj = btok2[j], xtok2[j]
            wait("sync", sT, st["trfree"][j][0])
            wait("sync", sV, st["trfree"][j][1])
            for nci in range(8):
                k = half * 8 + nci
                st["trX"][j] = dma_tr(
                    xtj[:, nci * 128:(nci + 1) * 128],
                    xact_sp[k][:, t0:t0 + 128], "trX")
            wait("sync", sS, st["nbds"][rp])
            for nci in range(8):
                st["trB"][j] = dma_tr(
                    btj[:, nci * 128:(nci + 1) * 128],
                    bds[:, nci * 512 + c * 128: nci * 512 + (c + 1) * 128],
                    "trB")
            wait("vector", sT, st["pe_par"])
            wait("vector", sS, st["wexp"])
            on("vector", lambda E, sl=sl: E.tensor_tensor_scan(
                out=lam[:], data0=dt_all[:, sl], data1=zeros32f[:],
                initial=0.0, op0=ALU.add, op1=ALU.add))
            on("vector", lambda E: E.tensor_scalar(
                out=lam[:], in0=lam[:], scalar1=a_col[:], scalar2=None,
                op0=ALU.mult))
            on("vector", lambda E, sl=sl: E.tensor_scalar(
                out=adt_hm[:], in0=dt_all[:, sl], scalar1=a_col[:],
                scalar2=None, op0=ALU.mult))
            on("vector", lambda E, sl=sl: E.tensor_copy(
                out=dthi_c[:], in_=dt_all[:, sl]), chain=False,
                inc=[(sV, 1)])
            nlam = sV.n
            wait("tensor", sV, nlam)
            bk = banks[0]
            bk.mm([lambda E: E.matmul(bk.ps[:, 0:128], adt_hm[:],
                                      ident_f[0:32, :], start=True, stop=True)])
            bk.read("vector", lambda E: E.tensor_copy(out=adtT[:],
                                                      in_=bk.ps[:, 0:32]))
            on("vector", lambda E: E.tensor_copy(out=adt_bf[:], in_=adtT[:]))
            on("vector", lambda E: E.tensor_copy(out=adthi_f[:], in_=adt_bf[:]))
            on("vector", lambda E: E.scalar_tensor_tensor(
                out=adtlo_f[:], in0=adthi_f[:], scalar=-1.0, in1=adtT[:],
                op0=ALU.mult, op1=ALU.add), inc=[(sV, 1)])
            nadt = sV.n
            for i in range(2):
                bk2 = banks[2 + i]
                bk2.mm([lambda E, i=i, half=half, bk2=bk2: E.matmul(
                    bk2.ps[:], dthi_c[:],
                    e32bf[:, half * 1024 + i * 512: half * 1024 + (i + 1) * 512],
                    start=True, stop=True)])
                bk2.read("vector", lambda E, i=i, bk2=bk2: E.tensor_copy(
                    out=dtbb[:, i * 512:(i + 1) * 512], in_=bk2.ps[:]),
                    chain=False)
            on("vector", lambda E: E.tensor_scalar(
                out=wvec[:], in0=lam[:], scalar1=lam[:, 127:128], scalar2=-1.0,
                op0=ALU.subtract, op1=ALU.mult))
            on("vector", lambda E, half=half: E.tensor_scalar(
                out=e8lam[:], in0=e8f[:, half * 8:(half + 1) * 8],
                scalar1=lam[:, 127:128], scalar2=None, op0=ALU.mult),
                chain=False, inc=[(sV, 1)])
            wait("scalar", sV, sV.n)
            wait("scalar", sT, st["ebfree"])
            on("scalar", lambda E: E.activation(out=wvec[:], in_=wvec[:],
                                                func=AF.Exp))
            on("scalar", lambda E: E.activation(out=explam_hm[:], in_=lam[:],
                                                func=AF.Exp), chain=False,
               inc=[(sS, 1)])
            st["wexp"] = sS.n
            wait("vector", sS, sS.n)
            on("vector", lambda E, sl=sl: E.tensor_tensor(
                out=wvec[:], in0=wvec[:], in1=dt_all[:, sl], op=ALU.mult))
            on("vector", lambda E: E.tensor_copy(out=wbf[:], in_=wvec[:]),
               inc=[(sV, 1)])
            nw = sV.n
            wait("tensor", sV, nw)
            bk1 = banks[1]
            bk1.mm([lambda E: E.matmul(bk1.ps[:, 0:8], pairself[:], e8lam[:],
                                       start=True, stop=True)])
            wait("scalar", sV, st["Ffree"][j])
            bk1.read("scalar", lambda E, j=j: E.activation(
                out=expLp2[j][:], in_=bk1.ps[:, 0:8], func=AF.Exp))
            st["nexpL"][j] = sS.n
            for i in range(2):
                bk2 = banks[2 + i]
                bk2.mm([lambda E, i=i, half=half, bk2=bk2: E.matmul(
                    bk2.ps[:], wbf[:],
                    e32bf[:, half * 1024 + i * 512: half * 1024 + (i + 1) * 512],
                    start=True, stop=True)])
                bk2.read("vector", lambda E, i=i, bk2=bk2: E.tensor_copy(
                    out=wtbb[:, i * 512:(i + 1) * 512], in_=bk2.ps[:]),
                    chain=False)
            dwait("vector", st["trX"][j])
            wait("vector", sT, st["trfree"][j][0])
            on("vector", lambda E, j=j, xtj=xtj: E.tensor_tensor(
                out=xd2[j][:], in0=xtj[:], in1=dtbb[:], op=ALU.mult))
            on("vector", lambda E, j=j, xtj=xtj: E.tensor_tensor(
                out=xw2[j][:], in0=xtj[:], in1=wtbb[:], op=ALU.mult))
            wait("vector", sS, st["ncds"][rp])
            for q8 in range(8):
                csl = slice(q8 * 512 + c * 128, q8 * 512 + (c + 1) * 128)
                on("vector", lambda E, q8=q8, csl=csl: E.tensor_copy(
                    out=cdsbd[0:64, q8 * 256: q8 * 256 + 128],
                    in_=cds[0:64, csl]), chain=(q8 == 0))
                on("vector", lambda E, q8=q8, csl=csl: E.tensor_copy(
                    out=cdsbd[64:128, q8 * 256 + 128: q8 * 256 + 256],
                    in_=cds[64:128, csl]), chain=False,
                    inc=[(sV, 1)] if q8 == 7 else None)
            nprep = sV.n

            wait("scalar", sT, st["pe_par"])
            wait("scalar", sV, nadt)
            for hl in range(16):
                on("scalar", lambda E, hl=hl, half=half: E.activation(
                    out=dtmgH[:, hl * 128:(hl + 1) * 128], in_=tri_up_f[:],
                    func=AF.Identity,
                    scale=adthi_f[:, half * 16 + hl: half * 16 + hl + 1]),
                    inc=[(sS, 1)] if hl == 15 else None, chain=(hl == 0))
            nhiS = sS.n
            for hl in range(16):
                on("vector", lambda E, hl=hl, half=half: E.tensor_scalar(
                    out=dtmgL[:, hl * 128:(hl + 1) * 128], in0=tri_up_f[:],
                    scalar1=adtlo_f[:, half * 16 + hl: half * 16 + hl + 1],
                    scalar2=None, op0=ALU.mult),
                    inc=[(sV, 1)] if hl == 15 else None, chain=(hl == 0))
            nlo = sV.n
            wait("tensor", sS, nhiS)
            wait("tensor", sV, nlo)
            for g in range(4):
                bg = banks[g]
                mms = [
                    lambda E, g=g, bg=bg: E.matmul(
                        bg.ps[:], tris_bf[:], dtmgH[:, g * 512:(g + 1) * 512],
                        start=True, stop=False),
                    lambda E, g=g, bg=bg: E.matmul(
                        bg.ps[:], tris_bf[:], dtmgL[:, g * 512:(g + 1) * 512],
                        start=False, stop=False),
                    lambda E, g=g, bg=bg: E.matmul(
                        bg.ps[:], ident_bf[:], postri4[:],
                        start=False, stop=True)]
                bg.mm(mms)
            wait("scalar", sV, st["nSM"])
            for g in range(4):
                banks[g].read("scalar", lambda E, g=g, bg=banks[g]: E.activation(
                    out=expDb[:, g * 512:(g + 1) * 512], in_=bg.ps[:],
                    func=AF.Exp), chain=(g == 0))

            wait("tensor", sV, nprep)
            wait("tensor", sS, st["ncds"][rp])
            for g in range(4):
                bg = banks[g]
                mms = []
                for pp in range(2):
                    p2 = g * 2 + pp
                    cs = slice(p2 * 512 + c * 128, p2 * 512 + (c + 1) * 128)
                    mms.append(lambda E, pp=pp, p2=p2, cs=cs, bg=bg: E.matmul(
                        bg.ps[:, pp * 256:(pp + 1) * 256],
                        bds[:, cs], cdsbd[:, p2 * 256:(p2 + 1) * 256],
                        start=True, stop=True))
                bg.mm(mms)
            for g in range(4):
                banks[g].read("vector", lambda E, g=g, j=j, bg=banks[g]:
                              E.tensor_tensor(
                                  out=smg2[j][:, g * 512:(g + 1) * 512],
                                  in0=bg.ps[:],
                                  in1=expDb[:, g * 512:(g + 1) * 512],
                                  op=ALU.mult), chain=(g == 0))
            st["nSM"] = sV.n
            st["nSMv"][j] = sV.n

            for bb_i, i in ((2, 0), (3, 1)):
                bg = banks[bb_i]
                bg.mm([lambda E, i=i, half=half, bg=bg: E.matmul(
                    bg.ps[:], explam_hm[:],
                    e32bf[:, half * 1024 + i * 512: half * 1024 + (i + 1) * 512],
                    start=True, stop=True)])
                bg.read("vector", lambda E, i=i, j=j, bg=bg: E.tensor_copy(
                    out=explamb2[j][:, i * 512:(i + 1) * 512], in_=bg.ps[:]),
                    chain=(i == 0))
            st["ebfree"] = sT.n
            st["pe_par"] = sT.n

        def ssd_ser(half, r, c, ystore_prev):
            j = c % 2
            cds = cds2[r % 2]
            t0 = r * 512 + c * 128
            btj, xtj = btok2[j], xtok2[j]
            wait("vector", sT, st["hfree"])
            on("vector", lambda E: E.tensor_copy(
                out=hbd[0:64, :].rearrange("p (q t) -> p q t", t=128)[:, :, 0:64],
                in_=hstate[0:64, :].rearrange("p (q t) -> p q t", t=64)))
            on("vector", lambda E: E.tensor_copy(
                out=hbd[64:128, :].rearrange("p (q t) -> p q t", t=128)[:, :, 64:128],
                in_=hstate[64:128, :].rearrange("p (q t) -> p q t", t=64)),
               inc=[(sV, 1)], chain=False)
            nhbd = sV.n
            wait("tensor", sV, nhbd)
            for bb_i, q0 in ((6, 0), (7, 4)):
                bg = banks[bb_i]
                mms = []
                for qq in range(4):
                    q = q0 + qq
                    cs = slice(q * 512 + c * 128, q * 512 + (c + 1) * 128)
                    mms.append(lambda E, qq=qq, cs=cs, q=q, cds=cds, bg=bg:
                               E.matmul(
                                   bg.ps[:, qq * 128:(qq + 1) * 128],
                                   cds[:, cs],
                                   hbd[:, q * 128:(q + 1) * 128],
                                   start=True, stop=True))
                bg.mm(mms)
            st["hfree"] = sT.n
            wait("tensor", sV, st["nSMv"][j])
            for bb_i, h0 in ((4, 0), (5, 8)):
                bg = banks[bb_i]
                mms = []
                for hh in range(8):
                    hl = h0 + hh
                    mms.append(lambda E, hh=hh, hl=hl, j=j, bg=bg: E.matmul(
                        bg.ps[:, hh * 64:(hh + 1) * 64],
                        smg2[j][:, hl * 128:(hl + 1) * 128],
                        xd2[j][:, hl * 64:(hl + 1) * 64],
                        start=True, stop=True))
                bg.mm(mms)
            dwait("tensor", st["trB"][j])
            for bb_i, q0 in ((0, 0), (1, 4)):
                bg = banks[bb_i]
                mms = []
                for qq in range(4):
                    q = q0 + qq
                    mms.append(lambda E, qq=qq, q=q, j=j, bg=bg, btj=btj:
                               E.matmul(
                                   bg.ps[:, qq * 128:(qq + 1) * 128],
                                   btj[:, q * 128:(q + 1) * 128],
                                   xw2[j][:, q * 128:(q + 1) * 128],
                                   start=True, stop=True))
                bg.mm(mms)
            st["trfree"][j][0] = sT.n

            if ystore_prev is not None:
                dwait("vector", ystore_prev)
            for bb_i, i in ((6, 0), (7, 1)):
                banks[bb_i].read("vector", lambda E, i=i, j=j, bg=banks[bb_i]:
                                 E.tensor_tensor(
                                     out=ybuf[:, i * 512:(i + 1) * 512],
                                     in0=bg.ps[:],
                                     in1=explamb2[j][:, i * 512:(i + 1) * 512],
                                     op=ALU.mult), chain=(i == 0))
            for bb_i, i in ((4, 0), (5, 1)):
                banks[bb_i].read("vector", lambda E, i=i, bg=banks[bb_i]:
                                 E.scalar_tensor_tensor(
                                     out=ybuf[:, i * 512:(i + 1) * 512],
                                     in0=bg.ps[:], scalar=1.0,
                                     in1=ybuf[:, i * 512:(i + 1) * 512],
                                     op0=ALU.mult, op1=ALU.add),
                                 chain=(i == 0))
            on("vector", lambda E, xtj=xtj: E.tensor_tensor(
                out=dxbuf[:], in0=xtj[:], in1=dbc_h[:], op=ALU.mult),
               chain=False)
            on("vector", lambda E: E.tensor_tensor(
                out=ybuf[:], in0=ybuf[:], in1=dxbuf[:], op=ALU.add),
                inc=[(sV, 1)])
            st["trfree"][j][1] = sV.n
            nyb = sV.n
            wait("gpsimd", sV, nyb)
            ystore = dma("gpsimd", y_sp[t0:t0 + 128,
                                        half * 1024:(half + 1) * 1024],
                         ybuf[:], "yst")

            wait("vector", sS, st["nexpL"][j])
            for bb_i, q0 in ((0, 0), (1, 4)):
                bg = banks[bb_i]
                for qq in range(4):
                    q = q0 + qq
                    for par in range(2):
                        po = par * 64
                        bg.read("vector", lambda E, qq=qq, q=q, po=po, j=j,
                                bg=bg: E.scalar_tensor_tensor(
                                    out=hstate[po:po + 64, q * 64:(q + 1) * 64],
                                    in0=hstate[po:po + 64, q * 64:(q + 1) * 64],
                                    scalar=expLp2[j][po:po + 64, q:q + 1],
                                    in1=bg.ps[po:po + 64,
                                              qq * 128 + po: qq * 128 + po + 64],
                                    op0=ALU.mult, op1=ALU.add),
                                chain=(qq == 0 and po == 0 and bb_i == 0))
            st["Ffree"][j] = sV.n
            return ystore

        def proj_dt(half, r):
            wait("gpsimd", sT, st["xarfree"])
            nxa = dma("gpsimd", xar[:].rearrange("p (k t) -> p k t", t=512),
                      xact_sp.rearrange("k p t -> p k t")[:, :, r * 512:(r + 1) * 512],
                      "xar")
            dwait("tensor", nxa)
            if half == 0:
                bk = banks[0]
                bk.mm([(lambda E, k=k: E.matmul(
                    bk.ps[0:32, :], dtwT[:, k * 32:(k + 1) * 32],
                    xar[:, k * 512:(k + 1) * 512],
                    start=(k == 0), stop=(k == 15))) for k in range(16)])
                sl = slice(r * 512, (r + 1) * 512)
                bk.read("vector", lambda E, sl=sl: E.tensor_scalar(
                    out=dt_all[:, sl], in0=bk.ps[0:32, :],
                    scalar1=dtb_col[:], scalar2=None, op0=ALU.add))
                on("vector", lambda E, sl=sl: E.tensor_scalar(
                    out=dt_tmp[:], in0=dt_all[:, sl], scalar1=0.0,
                    scalar2=None, op0=ALU.max), inc=[(sV, 1)])
                wait("scalar", sV, sV.n)
                on("scalar", lambda E, sl=sl: E.activation(
                    out=dt_all[:, sl], in_=dt_all[:, sl], func=AF.Abs))
                on("scalar", lambda E, sl=sl: E.activation(
                    out=dt_all[:, sl], in_=dt_all[:, sl], func=AF.Exp,
                    scale=-1.0))
                on("scalar", lambda E, sl=sl: E.activation(
                    out=dt_all[:, sl], in_=dt_all[:, sl], func=AF.Ln,
                    bias=1.0), inc=[(sS, 1)])
                wait("vector", sS, sS.n)
                on("vector", lambda E, sl=sl: E.tensor_tensor(
                    out=dt_all[:, sl], in0=dt_all[:, sl], in1=dt_tmp[:],
                    op=ALU.add), inc=[(sV, 1)])
                wait("tensor", sV, sV.n)

        def proj_bc(half, r, which):
            rp = r % 2
            wsb, dst = ((bwt_half, bds2[rp]) if which == "B"
                        else (cwt_half, cds2[rp]))
            if which == "B" and "trB" in dsems:
                wait("scalar", dsems["trB"], dsems["trB"].n)
            for nf in range(8):
                bb = banks[1 + nf % 2]
                nfg = half * 8 + nf
                bb.mm([(lambda E, wsb=wsb, nfg=nfg, k=k, bb=bb: E.matmul(
                    bb.ps[:],
                    wsb[:, k * 1024 + (nfg % 8) * 128:
                        k * 1024 + (nfg % 8 + 1) * 128],
                    xar[:, k * 512:(k + 1) * 512],
                    start=(k == 0), stop=(k == 15))) for k in range(16)])
                bb.read("scalar", lambda E, dst=dst, nf=nf, bb=bb:
                        E.activation(out=dst[:, nf * 512:(nf + 1) * 512],
                                     in_=bb.ps[:], func=AF.Identity),
                        chain=False)
            if which == "B":
                st["nbds"][rp] = sS.n
            else:
                st["ncds"][rp] = sS.n
                st["xarfree"] = sT.n

        for half in range(2):
            wait("gpsimd", sT, sT.n)
            for k in range(16):
                dma("gpsimd", bwt_half[:, k * 1024:(k + 1) * 1024],
                    bwt_sp[k][:, half * 1024:(half + 1) * 1024], "pmw")
                n_wl = dma("gpsimd", cwt_half[:, k * 1024:(k + 1) * 1024],
                           cwt_sp[k][:, half * 1024:(half + 1) * 1024], "pmw")
            nddl = dma("gpsimd", dbc_h[:],
                       dbc_sp[:, half * 1024:(half + 1) * 1024], "pmw")
            dwait("vector", nddl)
            on("vector", lambda E: E.memset(hstate[:], 0.0), inc=[(sV, 1)])
            dwait("tensor", n_wl)
            wait("tensor", sV, sV.n)
            st["xarfree"] = sT.n
            proj_dt(half, 0)
            proj_bc(half, 0, "B")
            proj_bc(half, 0, "C")
            ystore = None
            for r in range(NRUN):
                pend = None
                for c in range(4):
                    ssd_par(half, r, c)
                    if pend is not None:
                        ystore = ssd_ser(half, r, pend, ystore)
                    pend = c
                    if r + 1 < NRUN:
                        if c == 1:
                            proj_dt(half, r + 1)
                        elif c == 2:
                            proj_bc(half, r + 1, "B")
                        elif c == 3:
                            proj_bc(half, r + 1, "C")
                ystore = ssd_ser(half, r, 3, ystore)
            if dbg and half == 0:
                wait("sync", sV, sV.n)
                wait("sync", sS, sS.n)
                dma("sync", dbg_dt, dt_all[:], "dbg")

        dma_barrier()

        phase_reset("p4")
        woT = PH("woT", [128, 16 * 1024], BF16)
        lnw_row = PH("lnw_row", [1, DI], F32)
        lnb_row = PH("lnb_row", [1, DI], F32)
        lnw_bc = PH("lnw_bc", [128, DI], F32)
        lnb_bc = PH("lnb_bc", [128, DI], F32)
        p4y = [PH(f"p4y{i}", [128, DI], F32) for i in range(2)]
        p4z = [PH(f"p4z{i}", [128, DI], F32) for i in range(2)]
        p4r = [PH(f"p4r{i}", [128, DM], F32) for i in range(2)]
        p4w = [PH(f"p4w{i}", [128, DI], BF16) for i in range(2)]
        ynorm2 = [PH(f"ynorm{i}", [128, DI], F32) for i in range(2)]
        szb2 = [PH(f"szb{i}", [128, DI], F32) for i in range(2)]
        ygbf2 = [PH(f"ygbf{i}", [128, DI], BF16) for i in range(2)]
        ygT2 = [PH(f"ygT{i}", [128, DI], BF16) for i in range(2)]
        p4o = [PH(f"p4o{i}", [128, 512], F32) for i in range(2)]

        dma("gpsimd", lnw_row[:], ln_w[None, :], "p4s")
        nr = dma("gpsimd", lnb_row[:], ln_b[None, :], "p4s")
        dwait("tensor", nr)
        bcast_row_f32(lnw_row, lnw_bc, DI, lambda i: i % 2)
        bcast_row_f32(lnb_row, lnb_bc, DI, lambda i: 2 + i % 2)
        p4wfree = [(None, 0)] * 2
        for ob in range(8):
            wb = p4w[ob % 2]
            s, v = p4wfree[ob % 2]
            if s is not None:
                wait("gpsimd", s, v)
            nld = dma("gpsimd", wb[:], w_out[ob * 128:(ob + 1) * 128, :],
                      f"p4w{ob % 2}")
            dwait("tensor", nld)
            for k in range(16):
                transpose_to(wb[:, k * 128:(k + 1) * 128], banks[4 + k % 2],
                             lambda E, b, k=k, ob=ob: E.tensor_copy(
                                 out=woT[:, k * 1024 + ob * 128:
                                         k * 1024 + (ob + 1) * 128],
                                 in_=b.ps[:, 0:128]))
            p4wfree[ob % 2] = (sT, sT.n)
        yfree = [[] for _ in range(2)]
        for tt in range(NT):
            b = p4y[tt % 2]
            wait_all("gpsimd", yfree[tt % 2])
            nld = dma("gpsimd", b[:], y_sp[tt * 128:(tt + 1) * 128, :],
                      f"p4y{tt % 2}")
            dwait("vector", nld)
            dwait("scalar", nld)
            ln_stats_tile(b[:], DI, tt, st_sum, st_sq)
            yfree[tt % 2] = [(sS, sS.n), (sV, sV.n)]
        wait("vector", sS, sS.n)
        ln_finalize(DI)
        n4v, n4s = sV.n, sS.n
        zfree = [(None, 0)] * 2
        rfree = [(None, 0)] * 2
        ost = [None, None]
        ngate = [0, 0]
        p4ld = [None, None]

        def p4_head(tt):
            p = tt % 2
            yb, zb, rb = p4y[p], p4z[p], p4r[p]
            ynorm, szb, ygbf, ygT = ynorm2[p], szb2[p], ygbf2[p], ygT2[p]
            wait_all("gpsimd", yfree[p])
            nldy = dma("gpsimd", yb[:], y_sp[tt * 128:(tt + 1) * 128, :],
                       f"p4y{p}")
            s, v = zfree[p]
            if s is not None:
                wait("gpsimd", s, v)
            nldz = dma("gpsimd", zb[:], z_sp[tt * 128:(tt + 1) * 128, :],
                       f"p4z{p}")
            s, v = rfree[p]
            if s is not None:
                wait("gpsimd", s, v)
            nldr = dma("gpsimd", rb[:], x_in[tt * 128:(tt + 1) * 128, :],
                       f"p4r{p}")
            p4ld[p] = nldr
            dwait("scalar", nldy)
            wait("scalar", sV, n4v)
            wait("scalar", sV, ngate[p])
            on("scalar", lambda E, yb=yb, ynorm=ynorm, tt=tt: E.activation(
                out=ynorm[:], in_=yb[:], func=AF.Identity,
                scale=st_rstd[:, tt:tt + 1], bias=st_nb[:, tt:tt + 1]),
                inc=[(sS, 1)])
            yfree[p] = [(sS, sS.n)]
            wait("vector", sS, sS.n)
            on("vector", lambda E, ynorm=ynorm: E.tensor_tensor(
                out=ynorm[:], in0=ynorm[:], in1=lnw_bc[:], op=ALU.mult))
            on("vector", lambda E, ynorm=ynorm: E.tensor_tensor(
                out=ynorm[:], in0=ynorm[:], in1=lnb_bc[:], op=ALU.add),
                inc=[(sV, 1)])
            dwait("scalar", nldz)
            on("scalar", lambda E, zb=zb, szb=szb: E.activation(
                out=szb[:], in_=zb[:], func=AF.Silu), inc=[(sS, 1)])
            zfree[p] = (sS, sS.n)
            wait("vector", sS, sS.n)
            on("vector", lambda E, ynorm=ynorm, szb=szb, ygbf=ygbf:
               E.tensor_tensor(
                   out=ygbf[:], in0=ynorm[:], in1=szb[:], op=ALU.mult),
                inc=[(sV, 1)])
            ngate[p] = sV.n
            wait("tensor", sV, sV.n)
            for k in range(16):
                transpose_to(ygbf[:, k * 128:(k + 1) * 128], banks[k % 2],
                             lambda E, b, k=k, ygT=ygT: E.tensor_copy(
                                 out=ygT[:, k * 128:(k + 1) * 128],
                                 in_=b.ps[:, 0:128]))

        p4_head(0)
        for tt in range(NT):
            p = tt % 2
            rb = p4r[p]
            ygT = ygT2[p]
            nldr = p4ld[p]
            wait("tensor", sV, sV.n)
            dwait("tensor", nldr)
            for og in range(2):
                bb = banks[2 + og]
                bb.mm([(lambda E, k=k, og=og, bb=bb, ygT=ygT: E.matmul(
                    bb.ps[:], ygT[:, k * 128:(k + 1) * 128],
                    woT[:, k * 1024 + og * 512: k * 1024 + (og + 1) * 512],
                    start=(k == 0), stop=(k == 15))) for k in range(16)])
            if tt + 1 < NT:
                p4_head(tt + 1)
            for og in range(2):
                bb = banks[2 + og]
                obuf = p4o[og]
                if ost[og] is not None:
                    dwait("vector", ost[og])
                bb.read("vector", lambda E, rb=rb, og=og, obuf=obuf, bb=bb:
                        E.scalar_tensor_tensor(
                            out=obuf[:], in0=bb.ps[:], scalar=1.0,
                            in1=rb[:, og * 512:(og + 1) * 512],
                            op0=ALU.mult, op1=ALU.add))
                wait("sync", sV, sV.n)
                ost[og] = dma("sync", r1_sp[tt * 128:(tt + 1) * 128,
                                             og * 512:(og + 1) * 512], obuf[:],
                              f"r1st{og}")
            rfree[p] = (sV, sV.n)

        dma_barrier()

        phase_reset("p5")
        w1 = PH("w1", [128, 8 * 4096], BF16)
        w2 = PH("w2", [128, 32 * 1024], BF16)
        fgam_bc = PH("fgam_bc", [128, DM], F32)
        fbet_bc = PH("fbet_bc", [128, DM], F32)
        fc2b_rowbf = PH("fc2b_rowbf", [1, DM], BF16)
        p5o = [PH(f"p5o{i}", [128, 512], F32) for i in range(2)]
        ubuf = sq_scr[:, 1024:1536]
        hpre = sq_scr[:, 0:1024]
        hbt = PH("hbt", [128, DM], BF16)
        mark5 = ph_off[0]
        qsb = PH("qsb", [128, 1024], I32)
        scbc = PH("scbc", [128, 1024], F32)
        scrow = [PH("scrow0", [1, 1024], F32)]
        fg_row = PH("fg_row", [1, DM], F32)
        fb_row = PH("fb_row", [1, DM], F32)

        dma("gpsimd", fg_row[:], f_gam[None, :], "p5s")
        dma("gpsimd", fb_row[:], f_bet[None, :], "p5s")
        nr5 = dma("gpsimd", fc2b_rowbf[:], fc2_b[None, :], "p5s")
        dwait("tensor", nr5)
        bcast_row_f32(fg_row, fgam_bc, DM, lambda i: i % 2)
        bcast_row_f32(fb_row, fbet_bc, DM, lambda i: i % 2)
        qfree = (None, 0)
        srfree = [0, 0]
        for (qt, sct, wt, nk, wid) in ((fc1_q, fc1_s, w1, 8, 4096),
                                       (fc2_q, fc2_s, w2, 32, DM)):
            for k in range(nk):
                for hf in range(wid // 1024):
                    hw = 1024
                    s, v = qfree
                    if s is not None:
                        wait("gpsimd", s, v)
                    nld = dma("gpsimd", qsb[:, 0:hw],
                              qt[k * 128:(k + 1) * 128,
                                 hf * 1024: hf * 1024 + hw], "qsb")
                    srb = scrow[0]
                    wait("gpsimd", sT, srfree[0])
                    nlds = dma("gpsimd", srb[0:1, 0:hw],
                               sct[k:k + 1, hf * 1024: hf * 1024 + hw], "scrow")
                    dwait("tensor", nlds)
                    for i in range(hw // 512):
                        bb = banks[i % 2]
                        bb.mm([lambda E, bb=bb, i=i, srb=srb:
                               E.matmul(bb.ps[:], ones_f[:],
                                        srb[0:1, i * 512:(i + 1) * 512],
                                        start=True, stop=True)])
                        bb.read("vector", lambda E, bb=bb, i=i: E.tensor_copy(
                            out=scbc[:, i * 512:(i + 1) * 512], in_=bb.ps[:]))
                    srfree[0] = sT.n
                    dwait("vector", nld)
                    on("vector", lambda E, k=k, hf=hf, hw=hw, wt=wt, wid=wid:
                       E.scalar_tensor_tensor(
                           out=wt[:, k * wid + hf * 1024:
                                  k * wid + hf * 1024 + hw],
                           in0=qsb[:, 0:hw], scalar=128.0, in1=scbc[:, 0:hw],
                           op0=ALU.subtract, op1=ALU.mult), inc=[(sV, 1)])
                    qfree = (sV, sV.n)
        n_deq = sV.n
        ph_off[0] = mark5
        p5r = [PH(f"p5r{i}", [128, DM], F32) for i in range(4)]
        hfm = PH("hfm", [128, 8 * 256], BF16)
        sfm = PH("sfm", [128, 32 * 256], BF16)
        wait("gpsimd", sV, n_deq)
        r5free = [[] for _ in range(3)]
        for tt in range(NT):
            b = p5r[tt % 3]
            wait_all("gpsimd", r5free[tt % 3])
            nld = dma("gpsimd", b[:], r1_sp[tt * 128:(tt + 1) * 128, :],
                      f"p5r{tt % 3}")
            dwait("vector", nld)
            dwait("scalar", nld)
            ln_stats_tile(b[:], DM, tt, st_sum, st_sq)
            r5free[tt % 3] = [(sS, sS.n), (sV, sV.n)]
        wait("vector", sS, sS.n)
        ln_finalize(DM, derive_u=True)
        n5v = sV.n
        for lst in r5free:
            lst.append((sV, n5v))
        rr_free = [list(r5free[i % 3]) for i in range(4)]
        o5st = [None, None]
        u_cons = [0]
        nhbt = [0]
        for r in range(2 * NRUN):
            rbs = []
            for tti in range(2):
                slot = (2 * r + tti) % 4
                b = p5r[slot]
                wait_all("gpsimd", rr_free[slot])
                tt = r * 2 + tti
                nld = dma("gpsimd", b[:], r1_sp[tt * 128:(tt + 1) * 128, :],
                          f"p5r{slot}")
                rbs.append((b, nld, slot))
            for tti, (b, nld, slot) in enumerate(rbs):
                tt = r * 2 + tti
                dwait("scalar", nld)
                wait("scalar", sV, n5v)
                wait("scalar", sV, nhbt[0])
                on("scalar", lambda E, b=b, tt=tt: E.activation(
                    out=hpre, in_=b[:], func=AF.Identity,
                    scale=st_comb[:, tt:tt + 1], bias=st_nbc[:, tt:tt + 1]),
                    inc=[(sS, 1)])
                wait("vector", sS, sS.n)
                on("vector", lambda E: E.tensor_tensor(
                    out=hpre, in0=hpre, in1=fgam_bc[:], op=ALU.mult))
                on("vector", lambda E: E.tensor_tensor(
                    out=hbt[:], in0=hpre, in1=fbet_bc[:], op=ALU.add),
                    inc=[(sV, 1)])
                nhbt[0] = sV.n
                wait("tensor", sV, sV.n)
                for k in range(8):
                    transpose_to(hbt[:, k * 128:(k + 1) * 128], banks[k % 2],
                                 lambda E, bk, k=k, tti=tti: E.tensor_copy(
                                     out=hfm[:, k * 256 + tti * 128:
                                             k * 256 + (tti + 1) * 128],
                                     in_=bk.ps[:, 0:128]))
            nhfm = sV.n
            wait("tensor", sV, nhfm)
            for fk in range(32):
                bb = banks[2 + fk % 2]
                bb.mm([(lambda E, fk=fk, k=k, bb=bb: E.matmul(
                    bb.ps[:, 0:256],
                    w1[:, k * 4096 + fk * 128: k * 4096 + (fk + 1) * 128],
                    hfm[:, k * 256:(k + 1) * 256],
                    start=(k == 0), stop=(k == 7))) for k in range(8)])
                bb.read("scalar", lambda E, fk=fk, bb=bb: E.activation(
                    out=sfm[:, fk * 256:(fk + 1) * 256], in_=bb.ps[:, 0:256],
                    func=AF.Silu, bias=fc1b_col[:, fk:fk + 1]), chain=(fk == 0))
            nsfm = sS.n
            wait("tensor", sS, nsfm)
            for tti, (b, nld, slot) in enumerate(rbs):
                tt = r * 2 + tti
                for og in range(2):
                    bb = banks[4 + og]
                    obuf = p5o[og]
                    if o5st[og] is not None:
                        dwait("vector", o5st[og])
                    mms = [(lambda E, fk=fk, tti=tti, og=og, bb=bb: E.matmul(
                        bb.ps[:],
                        sfm[:, fk * 256 + tti * 128: fk * 256 + (tti + 1) * 128],
                        w2[:, fk * 1024 + og * 512: fk * 1024 + (og + 1) * 512],
                        start=(fk == 0), stop=False)) for fk in range(32)]
                    mms.append(lambda E, og=og, bb=bb: E.matmul(
                        bb.ps[:], onesall_bf[0:1, :],
                        fc2b_rowbf[0:1, og * 512:(og + 1) * 512],
                        start=False, stop=True))
                    bb.mm(mms)
                    wait("scalar", sV, u_cons[0])
                    on("scalar", lambda E, b=b, tt=tt, og=og: E.activation(
                        out=ubuf, in_=b[:, og * 512:(og + 1) * 512],
                        func=AF.Identity, scale=st_rstd[:, tt:tt + 1],
                        bias=st_nb[:, tt:tt + 1]), inc=[(sS, 1)])
                    wait("vector", sS, sS.n)
                    bb.read("vector", lambda E, obuf=obuf, bb=bb: E.scalar_tensor_tensor(
                        out=obuf[:], in0=bb.ps[:], scalar=0.5, in1=ubuf,
                        op0=ALU.mult, op1=ALU.add))
                    u_cons[0] = sV.n
                    on("vector", lambda E, obuf=obuf, b=b, og=og: E.tensor_tensor(
                        out=obuf[:], in0=obuf[:],
                        in1=b[:, og * 512:(og + 1) * 512], op=ALU.add),
                        inc=[(sV, 1)])
                    wait("sync", sV, sV.n)
                    o5st[og] = dma("sync", r2_sp[tt * 128:(tt + 1) * 128,
                                                 og * 512:(og + 1) * 512],
                                   obuf[:], f"r2st{og}")
                rr_free[slot] = [(sV, sV.n)]

        dma_barrier()

        phase_reset("p6")
        wqT = PH("wqT", [128, 8 * 3072], BF16)
        wpT = PH("wpT", [128, 8 * 1024], BF16)
        bq_row = PH("bq_row", [1, 3 * DM], BF16)
        bp_row = PH("bp_row", [1, DM], BF16)
        e16f = PH("e16f", [16, 1024], F32)
        p6w = [PH(f"p6w{i}", [128, DM], BF16) for i in range(2)]
        p6r = [PH(f"p6r{i}", [128, DM], F32) for i in range(2)]
        xlnb2 = [PH(f"xlnb{i}", [128, DM], BF16) for i in range(2)]
        xfmb2 = [PH(f"xfmb{i}", [128, DM], BF16) for i in range(2)]
        qsc = PH("qsc", [128, DM], F32)
        ksc = PH("ksc", [128, DM], F32)
        vsc = PH("vsc", [128, DM], F32)
        tb1 = PH("tb1", [128, DM], F32)
        qe = PH("qe", [128, DM], F32)
        ke = PH("ke", [128, DM], F32)
        kvb = PH("kvb", [128, DM], F32)
        ke_bf = PH("ke_bf", [128, DM], BF16)
        kv_bf = PH("kv_bf", [128, DM], BF16)
        kc = PH("kc", [128, DM], F32)
        kvc = PH("kvc", [128, DM], F32)
        kc_bf = PH("kc_bf", [128, DM], BF16)
        kvc_bf = PH("kvc_bf", [128, DM], BF16)
        car_k = PH("car_k", [1, DM], BF16)
        car_kv = PH("car_kv", [1, DM], BF16)
        dnm = PH("dnm", [128, 16], F32)
        recT = PH("recT", [16, 128], F32)
        recb = PH("recb", [128, DM], F32)
        ybf6 = PH("ybf6", [128, DM], BF16)
        yfm6 = PH("yfm6", [128, DM], BF16)
        p6o = [PH(f"p6o{i}", [128, 512], F32) for i in range(2)]
        m_col = PH("m_col", [128, 1], F32)
        v_col = PH("v_col", [128, 1], F32)
        rs_col = PH("rs_col", [128, 1], F32)
        nb_col = PH("nb_col", [128, 1], F32)

        dma("gpsimd", bq_row[:], bqkv[None, :], "p6s")
        dma("gpsimd", bp_row[:], bproj[None, :], "p6s")
        n6s = dma("gpsimd", e16f[:], cns["c_e16_f"][:], "p6s")
        dwait("tensor", n6s)
        wfree6 = [(None, 0)] * 2
        for src, tgt, nob in ((wqkv, wqT, 24), (wproj, wpT, 8)):
            for ob in range(nob):
                wb = p6w[ob % 2]
                s, v = wfree6[ob % 2]
                if s is not None:
                    wait("gpsimd", s, v)
                nld = dma("gpsimd", wb[:], src[ob * 128:(ob + 1) * 128, :],
                          f"p6w{ob % 2}")
                dwait("tensor", nld)
                for k in range(8):
                    width = 3072 if tgt is wqT else 1024
                    transpose_to(wb[:, k * 128:(k + 1) * 128], banks[4 + k % 2],
                                 lambda E, b, k=k, ob=ob, tgt=tgt, width=width:
                                 E.tensor_copy(
                                     out=tgt[:, k * width + ob * 128:
                                             k * width + (ob + 1) * 128],
                                     in_=b.ps[:, 0:128]))
                wfree6[ob % 2] = (sT, sT.n)

        r6free = [[] for _ in range(2)]
        for tt in range(NT):
            rb = p6r[tt % 2]
            wait_all("gpsimd", r6free[tt % 2])
            nld = dma("gpsimd", rb[:], r2_sp[tt * 128:(tt + 1) * 128, :],
                      f"p6r{tt % 2}")
            dwait("vector", nld)
            dwait("scalar", nld)
            ln_stats_tile(rb[:], DM, tt, st_sum, st_sq)
            r6free[tt % 2] = [(sS, sS.n), (sV, sV.n)]
        wait("vector", sS, sS.n)
        ln_finalize(DM)
        n6v = sV.n
        o6st = [None, None]

        def p6_head(tt):
            p = tt % 2
            rb = p6r[tt % 2]
            xlnb, xfmb = xlnb2[p], xfmb2[p]
            wait_all("gpsimd", r6free[tt % 2])
            nld = dma("gpsimd", rb[:], r2_sp[tt * 128:(tt + 1) * 128, :],
                      f"p6r{tt % 2}")
            dwait("scalar", nld)
            dwait("vector", nld)
            wait("scalar", sV, n6v)
            on("scalar", lambda E, rb=rb, xlnb=xlnb, tt=tt: E.activation(
                out=xlnb[:], in_=rb[:], func=AF.Identity,
                scale=st_rstd[:, tt:tt + 1], bias=st_nb[:, tt:tt + 1]),
                inc=[(sS, 1)])
            wait("tensor", sS, sS.n)
            for k in range(8):
                transpose_to(xlnb[:, k * 128:(k + 1) * 128], banks[k % 2],
                             lambda E, b, k=k, xfmb=xfmb: E.tensor_copy(
                                 out=xfmb[:, k * 128:(k + 1) * 128],
                                 in_=b.ps[:, 0:128]))

        p6_head(0)
        for tt in range(NT):
            p = tt % 2
            rb = p6r[tt % 2]
            xfmb = xfmb2[p]
            wait("tensor", sV, sV.n)
            for og in range(6):
                bb = banks[2 + og % 4]
                mms = [(lambda E, k=k, og=og, bb=bb, xfmb=xfmb: E.matmul(
                    bb.ps[:], xfmb[:, k * 128:(k + 1) * 128],
                    wqT[:, k * 3072 + og * 512: k * 3072 + (og + 1) * 512],
                    start=(k == 0), stop=False)) for k in range(8)]
                mms.append(lambda E, og=og, bb=bb: E.matmul(
                    bb.ps[:], onesall_bf[0:1, :],
                    bq_row[0:1, og * 512:(og + 1) * 512], start=False, stop=True))
                bb.mm(mms)
                if og < 2:
                    bb.read("scalar", lambda E, og=og, bb=bb: E.activation(
                        out=qsc[:, og * 512:(og + 1) * 512], in_=bb.ps[:],
                        func=AF.Identity, scale=scale_col[:]), chain=(og == 0))
                elif og < 4:
                    bb.read("scalar", lambda E, og=og, bb=bb: E.activation(
                        out=ksc[:, (og - 2) * 512:(og - 1) * 512], in_=bb.ps[:],
                        func=AF.Identity, scale=scale_col[:]), chain=False)
                else:
                    bb.read("vector", lambda E, og=og, bb=bb: E.tensor_copy(
                        out=vsc[:, (og - 4) * 512:(og - 3) * 512], in_=bb.ps[:]),
                        chain=False)
            nqkv = sS.n
            if tt + 1 < NT:
                p6_head(tt + 1)
            wait("vector", sS, nqkv)
            for xin, xout in ((qsc, qe), (ksc, ke)):
                on("vector", lambda E, xin=xin: E.tensor_scalar(
                    out=tb1[:], in0=xin[:], scalar1=0.0, scalar2=None,
                    op0=ALU.min), inc=[(sV, 1)])
                wait("scalar", sV, sV.n)
                on("scalar", lambda E: E.activation(
                    out=tb1[:], in_=tb1[:], func=AF.Exp), inc=[(sS, 1)])
                wait("vector", sS, sS.n)
                on("vector", lambda E, xin=xin, xout=xout: E.scalar_tensor_tensor(
                    out=xout[:], in0=xin[:], scalar=0.0, in1=tb1[:],
                    op0=ALU.max, op1=ALU.add), inc=[(sV, 1)])
            on("vector", lambda E: E.tensor_tensor(
                out=kvb[:], in0=ke[:], in1=vsc[:], op=ALU.mult))
            on("vector", lambda E: E.tensor_copy(out=ke_bf[:], in_=ke[:]))
            on("vector", lambda E: E.tensor_copy(out=kv_bf[:], in_=kvb[:]),
               inc=[(sV, 1)])
            ncast = sV.n
            wait("tensor", sV, ncast)
            for (data_bf, carrow, acc) in (
                    (ke_bf, car_k, kc), (kv_bf, car_kv, kvc)):
                for og in range(2):
                    bb = banks[4 + og]
                    mms = [lambda E, data_bf=data_bf, og=og, bb=bb, tt=tt:
                           E.matmul(bb.ps[:], tri_up_bf[:],
                                    data_bf[:, og * 512:(og + 1) * 512],
                                    start=True, stop=(tt == 0))]
                    if tt > 0:
                        mms.append(lambda E, carrow=carrow, og=og, bb=bb:
                                   E.matmul(bb.ps[:], onesall_bf[0:1, :],
                                            carrow[0:1, og * 512:(og + 1) * 512],
                                            start=False, stop=True))
                    bb.mm(mms)
                    bb.read("vector", lambda E, acc=acc, og=og, bb=bb:
                            E.tensor_copy(out=acc[:, og * 512:(og + 1) * 512],
                                          in_=bb.ps[:]), chain=False)
            on("vector", lambda E: E.tensor_copy(out=kc_bf[:], in_=kc[:]))
            on("vector", lambda E: E.tensor_copy(out=kvc_bf[:], in_=kvc[:]),
               inc=[(sV, 1)])
            wait("tensor", sV, sV.n)
            if tt < NT - 1:
                for (cb, carrow) in ((kc_bf, car_k), (kvc_bf, car_kv)):
                    for og in range(2):
                        b7 = banks[7]
                        b7.mm([lambda E, cb=cb, og=og, b7=b7: E.matmul(
                            b7.ps[0:1, 0:512], e127_bf[:],
                            cb[:, og * 512:(og + 1) * 512],
                            start=True, stop=True)])
                        b7.read("vector", lambda E, carrow=carrow, og=og, b7=b7:
                                E.tensor_copy(
                                    out=carrow[0:1, og * 512:(og + 1) * 512],
                                    in_=b7.ps[0:1, 0:512]), chain=False)
            on("vector", lambda E: E.tensor_tensor(
                out=tb1[:], in0=qe[:], in1=kc[:], op=ALU.mult), inc=[(sV, 1)])
            for hh in range(16):
                on("vector", lambda E, hh=hh: E.tensor_reduce(
                    out=dnm[:, hh:hh + 1], in_=tb1[:, hh * 64:(hh + 1) * 64],
                    axis=AX.X, op=ALU.add), inc=[(sV, 1)] if hh == 15 else None,
                    chain=(hh == 0))
            on("vector", lambda E: E.tensor_scalar(
                out=dnm[:], in0=dnm[:], scalar1=1e-3, scalar2=None, op0=ALU.max))
            on("vector", lambda E: E.reciprocal(out=dnm[:], in_=dnm[:]),
               inc=[(sV, 1)])
            nrec = sV.n
            wait("tensor", sV, nrec)
            b6 = banks[6]
            b6.mm([lambda E, b6=b6: E.matmul(b6.ps[0:16, 0:128], dnm[:],
                                             ident_f[:], start=True, stop=True)])
            b6.read("vector", lambda E, b6=b6: E.tensor_copy(
                out=recT[:], in_=b6.ps[0:16, 0:128]))
            wait("tensor", sV, sV.n)
            for og in range(2):
                bb = banks[4 + og]
                bb.mm([lambda E, og=og, bb=bb: E.matmul(
                    bb.ps[:], recT[:], e16f[:, og * 512:(og + 1) * 512],
                    start=True, stop=True)])
                bb.read("vector", lambda E, og=og, bb=bb: E.tensor_copy(
                    out=recb[:, og * 512:(og + 1) * 512], in_=bb.ps[:]),
                    chain=False)
            on("vector", lambda E: E.tensor_tensor(
                out=tb1[:], in0=qe[:], in1=kvc[:], op=ALU.mult))
            on("vector", lambda E: E.tensor_tensor(
                out=ybf6[:], in0=tb1[:], in1=recb[:], op=ALU.mult),
                inc=[(sV, 1)])
            wait("tensor", sV, sV.n)
            for k in range(8):
                transpose_to(ybf6[:, k * 128:(k + 1) * 128], banks[k % 2],
                             lambda E, b, k=k: E.tensor_copy(
                                 out=yfm6[:, k * 128:(k + 1) * 128],
                                 in_=b.ps[:, 0:128]))
            wait("tensor", sV, sV.n)
            for og in range(2):
                bb = banks[2 + og]
                obuf = p6o[og]
                if o6st[og] is not None:
                    dwait("vector", o6st[og])
                mms = [(lambda E, k=k, og=og, bb=bb: E.matmul(
                    bb.ps[:], yfm6[:, k * 128:(k + 1) * 128],
                    wpT[:, k * 1024 + og * 512: k * 1024 + (og + 1) * 512],
                    start=(k == 0), stop=False)) for k in range(8)]
                mms.append(lambda E, og=og, bb=bb: E.matmul(
                    bb.ps[:], onesall_bf[0:1, :],
                    bp_row[0:1, og * 512:(og + 1) * 512], start=False, stop=True))
                bb.mm(mms)
                bb.read("vector", lambda E, rb=rb, og=og, obuf=obuf, bb=bb:
                        E.scalar_tensor_tensor(
                            out=obuf[:], in0=bb.ps[:], scalar=1.0,
                            in1=rb[:, og * 512:(og + 1) * 512],
                            op0=ALU.mult, op1=ALU.add))
                wait("sync", sV, sV.n)
                o6st[og] = dma("sync", r3_sp[tt * 128:(tt + 1) * 128,
                                             og * 512:(og + 1) * 512], obuf[:],
                               f"r3st{og}")
            r6free[tt % 2] = [(sV, sV.n), (sS, sS.n)]

        dma_barrier()

        phase_reset("p7")
        weT = PH("weT", [128, 8 * 1024], BF16)
        dsh_bf = PH("dsh_bf", [128, DM], BF16)
        p7w = [PH(f"p7w{i}", [128, DM], BF16) for i in range(2)]
        p7r = [PH(f"p7r{i}", [128, DM], F32) for i in range(2)]
        xln7 = PH("xln7", [128, DM], BF16)
        xfm7 = PH("xfm7", [128, DM], BF16)
        p7o = [PH(f"p7o{i}", [128, 512], F32) for i in range(2)]

        dma("gpsimd", dsh_bf[0:64, :], disha[:], "p7s")
        ndl = dma("gpsimd", dsh_bf[64:128, :], disha[:], "p7s")
        dwait("vector", ndl)
        ndsh = sV.n
        wfree7 = [(None, 0)] * 2
        for ob in range(8):
            wb = p7w[ob % 2]
            s, v = wfree7[ob % 2]
            if s is not None:
                wait("gpsimd", s, v)
            nld = dma("gpsimd", wb[:], l1w[ob * 128:(ob + 1) * 128, :],
                      f"p7w{ob % 2}")
            dwait("tensor", nld)
            for k in range(8):
                transpose_to(wb[:, k * 128:(k + 1) * 128], banks[4 + k % 2],
                             lambda E, b, k=k, ob=ob: E.tensor_copy(
                                 out=weT[:, k * 1024 + ob * 128:
                                         k * 1024 + (ob + 1) * 128],
                                 in_=b.ps[:, 0:128]))
            wfree7[ob % 2] = (sT, sT.n)
        for k in range(8):
            on("vector", lambda E, k=k: E.tensor_tensor(
                out=weT[:, k * 1024:(k + 1) * 1024],
                in0=weT[:, k * 1024:(k + 1) * 1024],
                in1=dsh_bf[:], op=ALU.add),
                inc=[(sV, 1)] if k == 7 else None)
        nwe = sV.n
        wait("tensor", sV, nwe)

        r7free = [[] for _ in range(2)]
        for tt in range(NT):
            rb = p7r[tt % 2]
            wait_all("gpsimd", r7free[tt % 2])
            nld = dma("gpsimd", rb[:], r3_sp[tt * 128:(tt + 1) * 128, :],
                      f"p7r{tt % 2}")
            dwait("vector", nld)
            dwait("scalar", nld)
            ln_stats_tile(rb[:], DM, tt, st_sum, st_sq)
            r7free[tt % 2] = [(sS, sS.n), (sV, sV.n)]
        wait("vector", sS, sS.n)
        ln_finalize(DM)
        n7v = sV.n
        o7st = [None, None]
        xln72 = [PH(f"xln7{i}", [128, DM], BF16) for i in range(2)]
        xfm72 = [PH(f"xfm7{i}", [128, DM], BF16) for i in range(2)]
        p7tr = [0, 0]

        def p7_head(tt):
            p = tt % 2
            rb = p7r[p]
            xln7, xfm7 = xln72[p], xfm72[p]
            wait_all("gpsimd", r7free[p])
            nld = dma("gpsimd", rb[:], r3_sp[tt * 128:(tt + 1) * 128, :],
                      f"p7r{p}")
            dwait("scalar", nld)
            dwait("vector", nld)
            wait("scalar", sV, n7v)
            wait("scalar", sT, p7tr[p])
            on("scalar", lambda E, rb=rb, xln7=xln7, tt=tt: E.activation(
                out=xln7[:], in_=rb[:], func=AF.Identity,
                scale=st_rstd[:, tt:tt + 1], bias=st_nb[:, tt:tt + 1]),
                inc=[(sS, 1)])
            wait("tensor", sS, sS.n)
            for k in range(8):
                transpose_to(xln7[:, k * 128:(k + 1) * 128], banks[k % 2],
                             lambda E, b, k=k, xfm7=xfm7: E.tensor_copy(
                                 out=xfm7[:, k * 128:(k + 1) * 128],
                                 in_=b.ps[:, 0:128]))
            p7tr[p] = sT.n

        p7_head(0)
        for tt in range(NT):
            p = tt % 2
            rb = p7r[p]
            xfm7 = xfm72[p]
            wait("tensor", sV, sV.n)
            for og in range(2):
                bb = banks[2 + og]
                obuf = p7o[og]
                if o7st[og] is not None:
                    dwait("vector", o7st[og])
                bb.mm([(lambda E, k=k, og=og, bb=bb, xfm7=xfm7: E.matmul(
                    bb.ps[:], xfm7[:, k * 128:(k + 1) * 128],
                    weT[:, k * 1024 + og * 512: k * 1024 + (og + 1) * 512],
                    start=(k == 0), stop=(k == 7))) for k in range(8)])
                bb.read("vector", lambda E, rb=rb, og=og, obuf=obuf, bb=bb:
                        E.scalar_tensor_tensor(
                            out=obuf[:], in0=bb.ps[:], scalar=1.0,
                            in1=rb[:, og * 512:(og + 1) * 512],
                            op0=ALU.mult, op1=ALU.add))
                wait("sync", sV, sV.n)
                o7st[og] = dma("sync", out[tt * 128:(tt + 1) * 128,
                                           og * 512:(og + 1) * 512], obuf[:],
                               f"outst{og}")
            if tt + 1 < NT:
                p7_head(tt + 1)
            r7free[p] = [(sV, sV.n), (sS, sS.n)]

        for s_ in dsems.values():
            wait("gpsimd", s_, s_.n)

        with nc.Block() as block:
            for eng in ENGINES:
                dec = getattr(block, eng)

                def mk(e):
                    def body(E):
                        pend = False
                        for fn, incs, chain, is_op in q[e]:
                            if chain and pend:
                                E.drain()
                                pend = False
                            rr = fn(E)
                            for s, k in incs:
                                rr.then_inc(s.h, k)
                            if is_op and e in ("vector", "scalar"):
                                pend = True
                    return body

                dec(mk(eng))
    if os.environ.get("KARENA"):
        print("SBUF base", _base, "limit", SBUF_LIMIT, "resident",
              ph_base[0] - _base,
              {k: v - ph_base[0] for k, v in ph_peak.items()})
    return nc


def run_raw(inputs, dbg=False, cores=8):
    cons = host_constants()
    nc = build(dbg=dbg)
    in_maps = []
    for c in range(8):
        m = {
            "x": np.ascontiguousarray(inputs["L0_attn_hidden_states"][c]),
            "in_proj_w": inputs["L0_attn_in_proj_w"],
            "conv_w": np.ascontiguousarray(
                np.asarray(inputs["L0_attn_conv_w"]).reshape(DI, 4)),
            "conv_b": inputs["L0_attn_conv_b"],
            "A_log": inputs["L0_attn_A_log"],
            "dt_w": inputs["L0_attn_dt_w"],
            "dt_b": inputs["L0_attn_dt_b"],
            "B_w": inputs["L0_attn_B_w"],
            "C_w": inputs["L0_attn_C_w"],
            "D": inputs["L0_attn_D"],
            "ln_w": inputs["L0_attn_ln_w"],
            "ln_b": inputs["L0_attn_ln_b"],
            "out_proj_w": inputs["L0_attn_out_proj_w"],
            "ffn_ln_gamma": inputs["L0_ffn_ln_gamma"],
            "ffn_ln_beta": inputs["L0_ffn_ln_beta"],
            "fc1_q": inputs["L0_ffn_fc1_q"],
            "fc1_scales": inputs["L0_ffn_fc1_scales"],
            "fc1_bias": inputs["L0_ffn_fc1_bias"],
            "fc2_q": inputs["L0_ffn_fc2_q"],
            "fc2_scales": inputs["L0_ffn_fc2_scales"],
            "fc2_bias": inputs["L0_ffn_fc2_bias"],
            "Wqkv": inputs["L1_attn_Wqkv"],
            "bqkv": inputs["L1_attn_bqkv"],
            "Wproj": inputs["L1_attn_Wproj"],
            "bproj": inputs["L1_attn_bproj"],
            "scale": np.asarray(inputs["L1_attn_scale"],
                                dtype=np.float32).reshape(1, 1),
            "L1_weight": inputs["L1_ffn_weight"],
            "disha": inputs["L1_ffn_disha"],
        }
        m.update(cons)
        in_maps.append(m)
    res = run_bass_kernel_spmd(nc, in_maps[:cores], core_ids=list(range(cores)))
    return res.results


def kernel(**inputs):
    results = run_raw(inputs, dbg=_DBG)
    return np.stack([np.asarray(r["out"], dtype=np.float32)
                     for r in results], axis=0)


if __name__ == "__main__":
    nc = build()
    print("build ok")
